# version 01
# baseline (speedup 1.0000x reference)
import os
import numpy as np
from contextlib import ExitStack

import concourse.bass as bass
import concourse.mybir as mybir
from concourse.bass_utils import run_bass_kernel_spmd

F32 = mybir.dt.float32
BF16 = mybir.dt.bfloat16
I32 = mybir.dt.int32
AF = mybir.ActivationFunctionType
ALU = mybir.AluOpType
AX = mybir.AxisListType

ENG2DEC = {"pe": "tensor", "dve": "vector", "act": "scalar", "pool": "gpsimd", "sp": "sync"}

D = 1024
NH = 16
SEQLEN = 2048
CPS = SEQLEN // 128
EPS = 1e-5
QEPS = 1e-8
NCORES = 8
BSTOP = int(os.environ.get("BSTOP", "99"))


class Prog:

    def __init__(self, nc, stk):
        self.nc = nc
        self.stk = stk
        self.sems = {}
        self.cnt = {}
        self.wm = {e: {} for e in ENG2DEC}
        self.writer = {}
        self.readers = {}
        self.pending = []

    def _sem(self, key):
        if key not in self.sems:
            self.sems[key] = self.stk.enter_context(self.nc.semaphore("S_" + key))
            self.cnt[key] = 0
        return self.sems[key]

    def _buf(self, eng):
        if not self.pending or self.pending[-1][0] != eng:
            self.pending.append([eng, []])
        return self.pending[-1][1]

    def emit(self, eng, fn, reads=(), writes=(), semkey=None, inc=1, extra_waits=()):
        semkey = semkey or eng
        self._sem(semkey)
        waits = []
        wm = self.wm[eng]

        def need(dep):
            sk, c = dep
            if wm.get(sk, 0) < c:
                waits.append((sk, c))
                wm[sk] = c

        for r in reads:
            if r in self.writer:
                need(self.writer[r])
        for w in writes:
            if w in self.writer:
                need(self.writer[w])
            for sk, c in self.readers.get(w, {}).items():
                need((sk, c))
        for d in extra_waits:
            need(d)
        if inc == 16 and self.cnt[semkey] > 0:
            need((semkey, self.cnt[semkey]))
        self.cnt[semkey] += inc
        mycnt = self.cnt[semkey]
        sem = self.sems[semkey]
        sems = self.sems

        def thunk(e, waits=waits, fn=fn, sem=sem, inc=inc):
            for sk, v in waits:
                e.wait_ge(sems[sk], v)
            fn(e).then_inc(sem, inc)

        self._buf(eng).append(thunk)
        for w in writes:
            self.writer[w] = (semkey, mycnt)
            self.readers[w] = {}
        for r in reads:
            self.readers.setdefault(r, {})[semkey] = mycnt
        return (semkey, mycnt)

    def claim_edges(self, region):
        deps = []
        if region in self.writer:
            deps.append(self.writer[region])
        deps.extend(self.readers.get(region, {}).items())
        return deps

    def barrier_wait(self, eng, dep):
        sems = self.sems
        sk, v = dep
        self._buf(eng).append(lambda e: e.wait_ge(sems[sk], v))

    def final_dma_waits(self, eng):
        for sk, c in sorted(self.cnt.items()):
            if c and sk not in ENG2DEC:
                self.barrier_wait(eng, (sk, c))

    def finalize(self, block):
        for eng, thunks in self.pending:
            dec = getattr(block, ENG2DEC[eng])

            def body(e, thunks=thunks):
                for t in thunks:
                    t(e)

            dec(body)
        self.pending = []


def build_nc(nseq=4, parts=("A", "B1", "B2"), climit=None, startup_only=False):
    nc = bass.Bass("TRN2", target_bir_lowering=False)
    NCH = nseq * CPS

    xin = nc.dram_tensor("xin", [NCH * 128, D], F32, kind="ExternalInput").ap()
    w_d = {nm: nc.dram_tensor(nm, [D, D], F32, kind="ExternalInput").ap()
           for nm in ["wq", "wk", "wv", "wu", "wo", "fw"]}
    r_d = {nm: nc.dram_tensor(nm, [1, D], F32, kind="ExternalInput").ap()
           for nm in ["bq", "bk", "bv", "bu", "bo", "lnw", "lnb", "fb"]}
    xout = nc.dram_tensor("xout", [NCH * 128, D], F32, kind="ExternalOutput").ap()

    x1_sp = nc.dram_tensor("x1_sp", [NCH * 128, D], F32, kind="Internal").ap()
    yth_sp = nc.dram_tensor("yth_sp", [NCH, 128, 8 * 128], BF16, kind="Internal").ap()
    ytl_sp = nc.dram_tensor("ytl_sp", [NCH, 128, 8 * 128], BF16, kind="Internal").ap()
    zth_sp = nc.dram_tensor("zth_sp", [NCH, 128, 8 * 128], BF16, kind="Internal").ap()
    ztl_sp = nc.dram_tensor("ztl_sp", [NCH, 128, 8 * 128], BF16, kind="Internal").ap()
    ws_sp = nc.dram_tensor("ws_sp", [8, 128], F32, kind="Internal").ap()

    with ExitStack() as stk:
        def sb(name, shape, dtype):
            return stk.enter_context(nc.sbuf_tensor(name, shape, dtype))

        wqT = sb("wqT", [128, 8, 8, 128], BF16)
        wbig = sb("wbig", [128, 4, 8 * D], BF16)

        ws_bc = sb("ws_bc", [128, D], F32)
        hb_bc = sb("hb_bc", [128, D], F32)
        g_bc = sb("g_bc", [128, D], F32)
        b_bc = sb("b_bc", [128, D], F32)
        brow_h = sb("brow_h", [1, 2 * D], BF16)
        brow_l = sb("brow_l", [1, 2 * D], BF16)
        ones_b = sb("ones_b", [1, 128], BF16)
        ones_f = sb("ones_f", [1, 128], F32)

        eps_c = sb("eps_c", [128, 1], F32)
        kvtmp = sb("kvtmp", [128, 512], F32)
        kvsgn = sb("kvsgn", [128, 512], F32)
        kvabs = sb("kvabs", [128, 512], F32)
        kv_h = sb("kv_h", [128, nseq, 2, 512], BF16)
        kv_l = sb("kv_l", [128, nseq, 2, 512], BF16)
        bdmask = sb("bdmask", [128, 1024], BF16)

        f = [sb(f"f{i}", [128, D], F32) for i in range(11)]
        wstage = f[10]
        rowtmp = f[9][0:1, :]
        i0 = sb("i0", [128, D], I32)
        bt = [sb(f"bt{i}", [128, D], BF16) for i in range(4)]
        T = [sb(f"T{i}", [128, 8, 128], BF16) for i in range(4)]
        L = [sb(f"L{i}", [128, 8 * 128], BF16) for i in range(4)]
        stat = [sb(f"stat{s}", [128, 12], F32) for s in (0, 1)]
        mv = [sb(f"mv{s}", [128, 2], F32) for s in (0, 1)]
        t1 = {k: [sb(f"t1{k}{s}", [128, 1], F32) for s in (0, 1)] for k in "abcdef"}

        ps1 = stk.enter_context(nc.psum_tensor("ps1", [128, D], F32))
        ps2 = stk.enter_context(nc.psum_tensor("ps2", [128, D], F32))
        ps3 = stk.enter_context(nc.psum_tensor("ps3", [128, D], F32))
        ps4 = stk.enter_context(nc.psum_tensor("ps4", [128, 512], F32))
        ps5 = stk.enter_context(nc.psum_tensor("ps5", [128, 512], F32))

        prog = Prog(nc, stk)
        block = stk.enter_context(nc.Block())
        E = prog.emit

        E("pool", lambda e: e.memset(ones_b[:], 1.0), writes=["ones_b"])
        E("pool", lambda e: e.memset(ones_f[:], 1.0), writes=["ones_f"])
        E("pool", lambda e: e.memset(eps_c[:], EPS), writes=["eps_c"])
        E("pool", lambda e: e.memset(bdmask[:], 0.0), writes=["bdmask"])
        for c in range(8):
            E("pool", lambda e, c=c: e.memset(bdmask[0:64, c * 128:c * 128 + 64], 1.0),
              reads=["bdmask"], writes=[f"bdm{c}a"])
            E("pool", lambda e, c=c: e.memset(bdmask[64:128, c * 128 + 64:c * 128 + 128], 1.0),
              reads=["bdmask"], writes=[f"bdm{c}b"])

        def load_rows(names):
            for i, nm in enumerate(names):
                E("act", lambda e, nm=nm: e.dma_start(out=rowtmp, in_=r_d[nm]),
                  writes=["f9"], semkey="lc", inc=16)
                E("act", lambda e, i=i: e.activation(out=brow_h[:, i * D:(i + 1) * D], in_=rowtmp,
                                                     func=AF.Copy, scale=1.0, bias=0.0),
                  reads=["f9"], writes=[f"browh{i}"])
                E("dve", lambda e, i=i: e.tensor_tensor(out=brow_l[:, i * D:(i + 1) * D], in0=rowtmp,
                                                        in1=brow_h[:, i * D:(i + 1) * D], op=ALU.subtract),
                  reads=["f9", f"browh{i}"], writes=[f"browl{i}"])

        def bcast(dst, dreg, src_d, scale):
            E("act", lambda e: e.dma_start(out=rowtmp, in_=src_d),
              writes=["f9"], semkey="lc", inc=16)
            for jh in range(2):
                ps = (ps4, ps5)[jh]
                preg = ("p4", "p5")[jh]
                E("pe", lambda e, jh=jh, ps=ps: e.matmul(ps[:], ones_f[:],
                                                         rowtmp[:, jh * 512:(jh + 1) * 512],
                                                         start=True, stop=True),
                  reads=["f9", "ones_f"], writes=[preg], extra_waits=prog.claim_edges(preg))
                E("act", lambda e, jh=jh, ps=ps, dst=dst, scale=scale:
                    e.activation(out=dst[:, jh * 512:(jh + 1) * 512], in_=ps[:], func=AF.Copy,
                                 scale=scale, bias=0.0),
                  reads=[preg], writes=[f"{dreg}{jh}"])

        bcast(g_bc, "g_bc", r_d["lnw"], 1.0)
        bcast(b_bc, "b_bc", r_d["lnb"], 1.0)
        bcast(hb_bc, "hb_bc", r_d["fb"], 0.5)

        wsc = sb("wsc", [128, 1], F32)
        wqs = sb("wqs", [128, 1], F32)
        for jc in range(8):
            E("act", lambda e, jc=jc: e.dma_start(out=wstage[:], in_=w_d["fw"][jc * 128:(jc + 1) * 128, :]),
              writes=["f10"], semkey="lw", inc=16)
            E("dve", lambda e: e.tensor_reduce(out=wsc[:], in_=wstage[:], axis=AX.X, op=ALU.max,
                                               apply_absolute_value=True),
              reads=["f10"], writes=["wsc"])
            E("dve", lambda e: e.tensor_scalar(out=wsc[:], in0=wsc[:], scalar1=QEPS, scalar2=None, op0=ALU.max),
              reads=["wsc"], writes=["wsc"])
            E("dve", lambda e: e.reciprocal(out=wqs[:], in_=wsc[:]), reads=["wsc"], writes=["wqs"])
            E("dve", lambda e: e.tensor_scalar(out=wqs[:], in0=wqs[:], scalar1=127.0, scalar2=None, op0=ALU.mult),
              reads=["wqs"], writes=["wqs"])
            E("act", lambda e: e.activation(out=f[4][:], in_=wstage[:], func=AF.Copy,
                                            scale=wqs[:, 0:1], bias=0.0),
              reads=["f10", "wqs"], writes=["f4"])
            E("act", lambda e: e.activation(out=i0[:], in_=f[4][:], func=AF.Copy, scale=1.0, bias=0.0),
              reads=["f4"], writes=["i0"])
            E("dve", lambda e: e.scalar_tensor_tensor(out=f[5][:], in0=f[4][:], scalar=-0.5, in1=i0[:],
                                                      op0=ALU.add, op1=ALU.is_ge),
              reads=["f4", "i0"], writes=["f5"])
            E("dve", lambda e: e.scalar_tensor_tensor(out=f[6][:], in0=f[4][:], scalar=0.5, in1=i0[:],
                                                      op0=ALU.add, op1=ALU.is_lt),
              reads=["f4", "i0"], writes=["f6"])
            E("dve", lambda e: e.tensor_tensor(out=f[3][:], in0=i0[:], in1=f[5][:], op=ALU.add),
              reads=["i0", "f5"], writes=["f3"])
            E("dve", lambda e: e.tensor_tensor(out=bt[0][:], in0=f[3][:], in1=f[6][:], op=ALU.subtract),
              reads=["f3", "f6"], writes=["bt0"])
            E("sp", lambda e, jc=jc: e.dma_start_transpose(wqT[:, :, jc, :], bt[0][:]),
              reads=["bt0"], writes=[f"wqT{jc}"], semkey="tr", inc=16)
            E("dve", lambda e: e.reciprocal(out=wsc[:], in_=wqs[:]), reads=["wqs"], writes=["wsc"])
            E("pool", lambda e, jc=jc: e.dma_start(out=ws_sp[jc], in_=wsc[:, 0]),
              reads=["wsc"], writes=[f"wsh{jc}"], semkey="sws", inc=16)
        E("act", lambda e: e.dma_start(out=rowtmp, in_=ws_sp.rearrange("c p -> (c p)").unsqueeze(0)),
          reads=[f"wsh{jc}" for jc in range(8)], writes=["f9"], semkey="lc", inc=16)
        for jh in range(2):
            ps = (ps4, ps5)[jh]
            preg = ("p4", "p5")[jh]
            E("pe", lambda e, jh=jh, ps=ps: e.matmul(ps[:], ones_f[:], rowtmp[:, jh * 512:(jh + 1) * 512],
                                                     start=True, stop=True),
              reads=["f9", "ones_f"], writes=[preg], extra_waits=prog.claim_edges(preg))
            E("act", lambda e, jh=jh, ps=ps: e.activation(out=ws_bc[:, jh * 512:(jh + 1) * 512], in_=ps[:],
                                                          func=AF.Copy, scale=1.0, bias=0.0),
              reads=[preg], writes=[f"ws_bc{jh}"])

        def load_w(nm, slot_hi, slot_lo):
            for c in range(8):
                E("act", lambda e, c=c, nm=nm: e.dma_start(
                    out=wstage[:], in_=w_d[nm][c * 128:(c + 1) * 128, :]),
                  writes=["f10"], semkey="lw", inc=16)
                E("act", lambda e, c=c, slot_hi=slot_hi: e.activation(
                    out=wbig[:, slot_hi, c * D:(c + 1) * D], in_=wstage[:], func=AF.Copy, scale=1.0, bias=0.0),
                  reads=["f10"], writes=[f"wslot{slot_hi}_{c}"])
                E("dve", lambda e, c=c, slot_hi=slot_hi, slot_lo=slot_lo: e.tensor_tensor(
                    out=wbig[:, slot_lo, c * D:(c + 1) * D], in0=wstage[:],
                    in1=wbig[:, slot_hi, c * D:(c + 1) * D], op=ALU.subtract),
                  reads=["f10", f"wslot{slot_hi}_{c}"], writes=[f"wslot{slot_lo}_{c}"])

        def ln_stats(sl, halves):
            E("dve", lambda e: e.bn_stats(out=stat[sl][:, 0:6], in_=halves[0][0]),
              reads=halves[0][1], writes=[f"stat{sl}a"])
            E("dve", lambda e: e.bn_stats(out=stat[sl][:, 6:12], in_=halves[1][0]),
              reads=halves[1][1], writes=[f"stat{sl}b"])
            E("dve", lambda e: e.bn_aggr(out=mv[sl][:], in_=stat[sl][:]),
              reads=[f"stat{sl}a", f"stat{sl}b"], writes=[f"mv{sl}"])
            E("act", lambda e: e.activation(out=t1["a"][sl][:], in_=mv[sl][:, 1:2], func=AF.Sqrt,
                                            scale=1.0, bias=eps_c[:, 0:1]),
              reads=[f"mv{sl}", "eps_c"], writes=[f"t1a{sl}"])
            E("dve", lambda e: e.reciprocal(out=t1["b"][sl][:], in_=t1["a"][sl][:]),
              reads=[f"t1a{sl}"], writes=[f"t1b{sl}"])
            E("dve", lambda e: e.scalar_tensor_tensor(out=t1["c"][sl][:], in0=mv[sl][:, 0:1], scalar=-1.0,
                                                      in1=t1["b"][sl][:], op0=ALU.mult, op1=ALU.mult),
              reads=[f"mv{sl}", f"t1b{sl}"], writes=[f"t1c{sl}"])

        def ln_apply(dst_tile, dst_reg, src_tile, src_regs, sl):
            E("act", lambda e: e.activation(out=dst_tile[:], in_=src_tile[:], func=AF.Identity,
                                            scale=t1["b"][sl][:, 0:1], bias=t1["c"][sl][:, 0:1]),
              reads=list(src_regs) + [f"t1b{sl}", f"t1c{sl}"], writes=[dst_reg])

        def quantize(sl, ln_tile, ln_reg, cq_tile, cq_reg, c1_tile, c1_reg, c2_tile, c2_reg,
                     xq_tile, xq_reg):
            E("dve", lambda e: e.tensor_reduce(out=t1["d"][sl][:], in_=ln_tile[:], axis=AX.X, op=ALU.max,
                                               apply_absolute_value=True),
              reads=[ln_reg], writes=[f"t1d{sl}"])
            E("dve", lambda e: e.tensor_scalar(out=t1["d"][sl][:], in0=t1["d"][sl][:], scalar1=QEPS,
                                               scalar2=None, op0=ALU.max),
              reads=[f"t1d{sl}"], writes=[f"t1d{sl}"])
            E("dve", lambda e: e.reciprocal(out=t1["e"][sl][:], in_=t1["d"][sl][:]),
              reads=[f"t1d{sl}"], writes=[f"t1e{sl}"])
            E("dve", lambda e: e.tensor_scalar(out=t1["e"][sl][:], in0=t1["e"][sl][:], scalar1=127.0,
                                               scalar2=None, op0=ALU.mult),
              reads=[f"t1e{sl}"], writes=[f"t1e{sl}"])
            E("act", lambda e: e.activation(out=cq_tile[:], in_=ln_tile[:], func=AF.Copy,
                                            scale=t1["e"][sl][:, 0:1], bias=0.0),
              reads=[ln_reg, f"t1e{sl}"], writes=[cq_reg])
            E("act", lambda e: e.activation(out=i0[:], in_=cq_tile[:], func=AF.Copy, scale=1.0, bias=0.0),
              reads=[cq_reg], writes=["i0"])
            E("dve", lambda e: e.scalar_tensor_tensor(out=c1_tile[:], in0=cq_tile[:], scalar=-0.5, in1=i0[:],
                                                      op0=ALU.add, op1=ALU.is_ge),
              reads=[cq_reg, "i0"], writes=[c1_reg])
            E("dve", lambda e: e.scalar_tensor_tensor(out=c2_tile[:], in0=cq_tile[:], scalar=0.5, in1=i0[:],
                                                      op0=ALU.add, op1=ALU.is_lt),
              reads=[cq_reg, "i0"], writes=[c2_reg])
            E("dve", lambda e: e.tensor_tensor(out=cq_tile[:], in0=i0[:], in1=c1_tile[:], op=ALU.add),
              reads=["i0", c1_reg], writes=[cq_reg]),
            E("dve", lambda e: e.tensor_tensor(out=xq_tile[:], in0=cq_tile[:], in1=c2_tile[:], op=ALU.subtract),
              reads=[cq_reg, c2_reg], writes=[xq_reg])
            E("dve", lambda e: e.reciprocal(out=t1["f"][sl][:], in_=t1["e"][sl][:]),
              reads=[f"t1e{sl}"], writes=[f"t1f{sl}"])
            E("dve", lambda e: e.tensor_scalar(out=t1["f"][sl][:], in0=t1["f"][sl][:], scalar1=0.5,
                                               scalar2=None, op0=ALU.mult),
              reads=[f"t1f{sl}"], writes=[f"t1f{sl}"])

        def ffn_mm(xqT_tile, xqT_reg, pslices, pregs):
            for jh in range(2):
                claims = prog.claim_edges(pregs[jh])
                for c in range(8):
                    E("pe", lambda e, c=c, jh=jh: e.matmul(
                        pslices[jh], xqT_tile[:, c, :],
                        wqT[:, c, :, :].rearrange("p jc j -> p (jc j)")[:, jh * 512:(jh + 1) * 512],
                        start=(c == 0), stop=(c == 7)),
                      reads=[xqT_reg] + [f"wqT{jc}" for jc in range(4 * jh, 4 * jh + 4)],
                      writes=[pregs[jh]] if c == 7 else [],
                      extra_waits=claims if c == 0 else ())

        def proj3(yhT_t, ylT_t, yregs, slots, brow_i, pslices, pregs):
            hi, lo = slots
            for jh in range(2):
                claims = prog.claim_edges(pregs[jh])
                passes = [(yhT_t, yregs[0], hi), (ylT_t, yregs[1], hi), (yhT_t, yregs[0], lo)]
                i_mm = 0
                for (ytile, yreg, slot) in passes:
                    for c in range(8):
                        E("pe", lambda e, c=c, jh=jh, slot=slot, ytile=ytile, i_mm=i_mm:
                            e.matmul(pslices[jh], ytile[:, c, :],
                                     wbig[:, slot, c * D + jh * 512: c * D + jh * 512 + 512],
                                     start=(i_mm == 0), stop=False),
                          reads=[yreg, f"wslot{slot}_{c}"], writes=[],
                          extra_waits=claims if i_mm == 0 else ())
                        i_mm += 1
                for bi, (brt, brg) in enumerate([(brow_h, f"browh{brow_i}"), (brow_l, f"browl{brow_i}")]):
                    last = (bi == 1)
                    E("pe", lambda e, jh=jh, brt=brt, last=last, brow_i=brow_i:
                        e.matmul(pslices[jh], ones_b[:],
                                 brt[:, brow_i * D + jh * 512: brow_i * D + jh * 512 + 512],
                                 start=False, stop=last),
                      reads=[brg, "ones_b"], writes=[pregs[jh]] if last else [])

        CL = climit or CPS
        if startup_only:
            prog.final_dma_waits("pool")
            prog.finalize(block)
            return nc
        load_rows(["bk", "bv"])
        load_w("wk", 0, 1)
        load_w("wv", 2, 3)
        for s in (range(nseq) if "A" in parts else []):
            for i in range(CL):
                g = s * CPS + i
                sl = g % 2
                xv = f[sl]
                E("act", lambda e, g=g, xv=xv, sl=sl: e.dma_start(out=xv[:], in_=xin[g * 128:(g + 1) * 128, :]),
                  writes=[f"f{sl}"], semkey=f"lx{sl}", inc=16)
                ln_stats(sl, [(xv[:, 0:512], [f"f{sl}"]), (xv[:, 512:1024], [f"f{sl}"])])
                ln_apply(f[3], "f3", xv, [f"f{sl}"], sl)
                quantize(sl, f[3], "f3", f[4], "f4", f[5], "f5", f[6], "f6", bt[0], "bt0")
                E("sp", lambda e: e.dma_start_transpose(T[0][:], bt[0][:]),
                  reads=["bt0"], writes=["T0"], semkey="tr", inc=16)
                ffn_mm(T[0], "T0", [ps1[:, 0:512], ps1[:, 512:1024]], ["p1a", "p1b"])
                for jh in range(2):
                    E("dve", lambda e, jh=jh, sl=sl: e.scalar_tensor_tensor(
                        out=f[7][:, jh * 512:(jh + 1) * 512], in0=ps1[:, jh * 512:(jh + 1) * 512],
                        scalar=t1["f"][sl][:, 0:1], in1=ws_bc[:, jh * 512:(jh + 1) * 512],
                        op0=ALU.mult, op1=ALU.mult),
                      reads=[("p1a", "p1b")[jh], f"t1f{sl}", f"ws_bc{jh}"], writes=[f"f7_{jh}"])
                E("pool", lambda e, xv=xv, sl=sl: e.tensor_tensor(out=f[8][:], in0=xv[:], in1=hb_bc[:],
                                                                  op=ALU.add),
                  reads=[f"f{sl}", "hb_bc0", "hb_bc1"], writes=["f8"])
                E("dve", lambda e: e.tensor_tensor(out=f[2][:], in0=f[7][:], in1=f[8][:], op=ALU.add),
                  reads=["f7_0", "f7_1", "f8"], writes=["f2"])
                E("pool", lambda e, g=g: e.dma_start(out=x1_sp[g * 128:(g + 1) * 128, :], in_=f[2][:]),
                  reads=["f2"], writes=[f"x1h{g}"], semkey=f"sx1{g % 2}", inc=16)
                ln_stats(sl, [(f[2][:, 0:512], ["f2"]), (f[2][:, 512:1024], ["f2"])])
                ln_apply(f[3], "f3", f[2], ["f2"], sl)
                E("act", lambda e: e.activation(out=bt[1][:], in_=f[3][:], func=AF.Copy, scale=1.0, bias=0.0),
                  reads=["f3"], writes=["bt1"])
                E("dve", lambda e: e.tensor_tensor(out=bt[2][:], in0=f[3][:], in1=bt[1][:], op=ALU.subtract),
                  reads=["f3", "bt1"], writes=["bt2"])
                E("sp", lambda e: e.dma_start_transpose(T[1][:], bt[1][:]),
                  reads=["bt1"], writes=["T1"], semkey="tr", inc=16)
                E("sp", lambda e: e.dma_start_transpose(T[2][:], bt[2][:]),
                  reads=["bt2"], writes=["T2"], semkey="tr", inc=16)
                E("pool", lambda e, g=g: e.dma_start(out=yth_sp[g], in_=T[1][:].rearrange("p c t -> p (c t)")),
                  reads=["T1"], writes=[f"yth{g}"], semkey=f"syh{g % 2}", inc=16)
                E("pool", lambda e, g=g: e.dma_start(out=ytl_sp[g], in_=T[2][:].rearrange("p c t -> p (c t)")),
                  reads=["T2"], writes=[f"ytl{g}"], semkey=f"syl{g % 2}", inc=16)
                proj3(T[1], T[2], ("T1", "T2"), (0, 1), 0,
                      [ps2[:, 0:512], ps2[:, 512:1024]], ["p2a", "p2b"])
                proj3(T[1], T[2], ("T1", "T2"), (2, 3), 1,
                      [ps3[:, 0:512], ps3[:, 512:1024]], ["p3a", "p3b"])
                E("act", lambda e: e.activation(out=f[9][:], in_=ps2[:], func=AF.Relu, scale=1.0, bias=0.0),
                  reads=["p2a", "p2b"], writes=["f9"])
                E("dve", lambda e: e.tensor_copy(out=f[10][:], in_=ps3[:]),
                  reads=["p3a", "p3b"], writes=["f10"])
                claims4 = prog.claim_edges("p4") if i == 0 else ()
                claims5 = prog.claim_edges("p5") if i == 0 else ()
                for c in range(8):
                    first = (i == 0 and c in (0, 4))
                    last = (i == CL - 1 and c in (3, 7))
                    psk = (ps4, ps5)[c // 4]
                    preg = ("p4", "p5")[c // 4]
                    fo = (c % 4) * 128
                    cl = (claims4, claims5)[c // 4]
                    E("pe", lambda e, c=c, first=first, last=last, fo=fo, psk=psk: e.matmul(
                        psk[:, fo:fo + 128],
                        f[9][:, c * 128:(c + 1) * 128], f[10][:, c * 128:(c + 1) * 128],
                        start=first, stop=last),
                      reads=["f9", "f10"],
                      writes=[preg] if last else [],
                      extra_waits=cl if first else ())
            maskregs = ["bdmask"] + [f"bdm{c}a" for c in range(8)] + [f"bdm{c}b" for c in range(8)]
            for bank in range(2):
                psk = (ps4, ps5)[bank]
                preg = ("p4", "p5")[bank]
                E("act", lambda e, psk=psk: e.activation(out=kvsgn[:], in_=psk[:], func=AF.Sign),
                  reads=[preg], writes=["kvsgn"])
                E("act", lambda e, psk=psk: e.activation(out=kvabs[:], in_=psk[:], func=AF.Abs),
                  reads=[preg], writes=["kvabs"])
                E("dve", lambda e: e.tensor_scalar(out=kvabs[:], in0=kvabs[:],
                                                   scalar1=0.01, scalar2=100.0,
                                                   op0=ALU.max, op1=ALU.min),
                  reads=["kvabs"], writes=["kvabs"])
                E("dve", lambda e, bank=bank: e.tensor_tensor(out=kvsgn[:], in0=kvsgn[:],
                                                              in1=bdmask[:, bank * 512:(bank + 1) * 512],
                                                              op=ALU.mult),
                  reads=["kvsgn"] + maskregs, writes=["kvsgn"])
                E("dve", lambda e: e.tensor_tensor(out=kvtmp[:], in0=kvabs[:], in1=kvsgn[:], op=ALU.mult),
                  reads=["kvabs", "kvsgn"], writes=["kvtmp"])
                E("act", lambda e, s=s, bank=bank: e.activation(out=kv_h[:, s, bank, :],
                                                                in_=kvtmp[:], func=AF.Copy,
                                                                scale=1.0, bias=0.0),
                  reads=["kvtmp"], writes=[f"kvh{s}_{bank}"])
                E("dve", lambda e, s=s, bank=bank: e.tensor_tensor(out=kv_l[:, s, bank, :],
                                                                   in0=kvtmp[:],
                                                                   in1=kv_h[:, s, bank, :], op=ALU.subtract),
                  reads=["kvtmp", f"kvh{s}_{bank}"], writes=[f"kvl{s}_{bank}"])

        load_rows(["bq", "bu"])
        load_w("wq", 0, 1)
        load_w("wu", 2, 3)
        for s in (range(nseq) if "B1" in parts else []):
            for i in range(CL):
                g = s * CPS + i
                sl = g % 2
                E("act", lambda e, g=g, sl=sl: e.dma_start(out=L[sl][:], in_=yth_sp[g]),
                  reads=[f"yth{g}"], writes=[f"L{sl}"], semkey=f"lyh{sl}", inc=16)
                E("act", lambda e, g=g, sl=sl: e.dma_start(out=L[2 + sl][:], in_=ytl_sp[g]),
                  reads=[f"ytl{g}"], writes=[f"L{2 + sl}"], semkey=f"lyl{sl}", inc=16)
                yhT_v = L[sl][:].rearrange("p (c t) -> p c t", c=8)
                ylT_v = L[2 + sl][:].rearrange("p (c t) -> p c t", c=8)
                if BSTOP < 2:
                    continue
                proj3(yhT_v, ylT_v, (f"L{sl}", f"L{2 + sl}"), (0, 1), 0,
                      [ps1[:, 0:512], ps1[:, 512:1024]], ["p1a", "p1b"])
                E("act", lambda e: e.activation(out=f[9][:], in_=ps1[:], func=AF.Relu, scale=1.0, bias=0.0),
                  reads=["p1a", "p1b"], writes=["f9"])
                if BSTOP < 3:
                    continue
                E("act", lambda e: e.activation(out=bt[0][:], in_=f[9][:], func=AF.Copy, scale=1.0, bias=0.0),
                  reads=["f9"], writes=["bt0"])
                E("dve", lambda e: e.tensor_tensor(out=bt[1][:], in0=f[9][:], in1=bt[0][:], op=ALU.subtract),
                  reads=["f9", "bt0"], writes=["bt1"])
                E("sp", lambda e: e.dma_start_transpose(T[0][:], bt[0][:]),
                  reads=["bt0"], writes=["T0"], semkey="tr", inc=16)
                E("sp", lambda e: e.dma_start_transpose(T[1][:], bt[1][:]),
                  reads=["bt1"], writes=["T1"], semkey="tr", inc=16)
                if BSTOP < 4:
                    continue
                proj3(yhT_v, ylT_v, (f"L{sl}", f"L{2 + sl}"), (2, 3), 1,
                      [ps2[:, 0:512], ps2[:, 512:1024]], ["p2a", "p2b"])
                E("dve", lambda e: e.tensor_copy(out=f[10][:], in_=ps2[:]),
                  reads=["p2a", "p2b"], writes=["f10"])
                if BSTOP < 5:
                    continue
                for bank in range(2):
                    breg = ("p3a", "p3b")[bank]
                    claims = prog.claim_edges(breg)
                    i_mm = 0
                    for c in range(bank * 4, bank * 4 + 4):
                        fo = (c % 4) * 128
                        for (qtile, qreg, kvt, kvtag) in (
                                (T[0], "T0", kv_h, "kvh"),
                                (T[1], "T1", kv_h, "kvh"),
                                (T[0], "T0", kv_l, "kvl")):
                            first = (i_mm == 0)
                            last = (i_mm == 11)
                            E("pe", lambda e, c=c, fo=fo, bank=bank, qtile=qtile, kvt=kvt,
                                first=first, last=last, s=s: e.matmul(
                                    ps3[:, c * 128:(c + 1) * 128],
                                    qtile[:, c, :],
                                    kvt[:, s, bank, fo:fo + 128],
                                    start=first, stop=last),
                              reads=[qreg, f"{kvtag}{s}_{bank}"],
                              writes=[breg] if last else [],
                              extra_waits=claims if first else ())
                            i_mm += 1
                if BSTOP < 6:
                    continue
                ln_stats(sl, [(ps3[:, 0:512], ["p3a"]), (ps3[:, 512:1024], ["p3b"])])
                E("act", lambda e, sl=sl: e.activation(out=f[3][:], in_=ps3[:], func=AF.Identity,
                                                       scale=t1["b"][sl][:, 0:1], bias=t1["c"][sl][:, 0:1]),
                  reads=["p3a", "p3b", f"t1b{sl}", f"t1c{sl}"], writes=["f3"])
                E("dve", lambda e: e.tensor_tensor(out=f[3][:], in0=f[3][:], in1=g_bc[:], op=ALU.mult),
                  reads=["f3", "g_bc0", "g_bc1"], writes=["f3"])
                E("dve", lambda e: e.tensor_tensor(out=f[3][:], in0=f[3][:], in1=b_bc[:], op=ALU.add),
                  reads=["f3", "b_bc0", "b_bc1"], writes=["f3"])
                E("dve", lambda e: e.tensor_tensor(out=f[3][:], in0=f[3][:], in1=f[10][:], op=ALU.mult),
                  reads=["f3", "f10"], writes=["f3"])
                E("act", lambda e: e.activation(out=bt[2][:], in_=f[3][:], func=AF.Copy, scale=1.0, bias=0.0),
                  reads=["f3"], writes=["bt2"])
                E("dve", lambda e: e.tensor_tensor(out=bt[3][:], in0=f[3][:], in1=bt[2][:], op=ALU.subtract),
                  reads=["f3", "bt2"], writes=["bt3"])
                if BSTOP < 7:
                    continue
                E("sp", lambda e: e.dma_start_transpose(T[2][:], bt[2][:]),
                  reads=["bt2"], writes=["T2"], semkey="tr", inc=16)
                E("sp", lambda e: e.dma_start_transpose(T[3][:], bt[3][:]),
                  reads=["bt3"], writes=["T3"], semkey="tr", inc=16)
                E("pool", lambda e, g=g: e.dma_start(out=zth_sp[g], in_=T[2][:].rearrange("p c t -> p (c t)")),
                  reads=["T2"], writes=[f"zth{g}"], semkey=f"szh{g % 2}", inc=16)
                E("pool", lambda e, g=g: e.dma_start(out=ztl_sp[g], in_=T[3][:].rearrange("p c t -> p (c t)")),
                  reads=["T3"], writes=[f"ztl{g}"], semkey=f"szl{g % 2}", inc=16)

        load_rows(["bo", "bo"])
        load_w("wo", 0, 1)
        for s in (range(nseq) if "B2" in parts else []):
            for i in range(CL):
                g = s * CPS + i
                sl = g % 2
                E("act", lambda e, g=g, sl=sl: e.dma_start(out=f[sl][:], in_=x1_sp[g * 128:(g + 1) * 128, :]),
                  reads=[f"x1h{g}"], writes=[f"f{sl}"], semkey=f"lx{sl}", inc=16)
                E("act", lambda e, g=g, sl=sl: e.dma_start(out=L[sl][:], in_=zth_sp[g]),
                  reads=[f"zth{g}"], writes=[f"L{sl}"], semkey=f"lyh{sl}", inc=16)
                E("act", lambda e, g=g, sl=sl: e.dma_start(out=L[2 + sl][:], in_=ztl_sp[g]),
                  reads=[f"ztl{g}"], writes=[f"L{2 + sl}"], semkey=f"lyl{sl}", inc=16)
                zhT_v = L[sl][:].rearrange("p (c t) -> p c t", c=8)
                zlT_v = L[2 + sl][:].rearrange("p (c t) -> p c t", c=8)
                proj3(zhT_v, zlT_v, (f"L{sl}", f"L{2 + sl}"), (0, 1), 0,
                      [ps4[:], ps5[:]], ["p4", "p5"])
                f2claims = prog.claim_edges("f2")
                for jh in range(2):
                    ps = (ps4, ps5)[jh]
                    E("dve", lambda e, jh=jh, ps=ps, sl=sl: e.tensor_tensor(
                        out=f[2][:, jh * 512:(jh + 1) * 512], in0=ps[:],
                        in1=f[sl][:, jh * 512:(jh + 1) * 512], op=ALU.add),
                      reads=[("p4", "p5")[jh], f"f{sl}"], writes=[f"f2_{jh}"],
                      extra_waits=f2claims)
                x2regs = ["f2_0", "f2_1"]
                ln_stats(sl, [(f[2][:, 0:512], [x2regs[0]]), (f[2][:, 512:1024], [x2regs[1]])])
                ln_apply(f[3], "f3", f[2], x2regs, sl)
                quantize(sl, f[3], "f3", f[4], "f4", f[5], "f5", f[6], "f6", bt[0], "bt0")
                E("sp", lambda e: e.dma_start_transpose(T[0][:], bt[0][:]),
                  reads=["bt0"], writes=["T0"], semkey="tr", inc=16)
                ffn_mm(T[0], "T0", [ps4[:], ps5[:]], ["p4", "p5"])
                for jh in range(2):
                    ps = (ps4, ps5)[jh]
                    E("dve", lambda e, jh=jh, ps=ps, sl=sl: e.scalar_tensor_tensor(
                        out=f[7][:, jh * 512:(jh + 1) * 512], in0=ps[:],
                        scalar=t1["f"][sl][:, 0:1], in1=ws_bc[:, jh * 512:(jh + 1) * 512],
                        op0=ALU.mult, op1=ALU.mult),
                      reads=[("p4", "p5")[jh], f"t1f{sl}", f"ws_bc{jh}"], writes=[f"f7_{jh}"])
                E("pool", lambda e: e.tensor_tensor(out=f[8][:], in0=f[2][:], in1=hb_bc[:], op=ALU.add),
                  reads=x2regs + ["hb_bc0", "hb_bc1"], writes=["f8"])
                E("dve", lambda e: e.tensor_tensor(out=f[9][:], in0=f[7][:], in1=f[8][:], op=ALU.add),
                  reads=["f7_0", "f7_1", "f8"], writes=["f9"])
                E("pool", lambda e, g=g: e.dma_start(out=xout[g * 128:(g + 1) * 128, :], in_=f[9][:]),
                  reads=["f9"], writes=[f"out{g}"], semkey=f"so{g % 2}", inc=16)

        prog.final_dma_waits("pool")
        prog.finalize(block)

    return nc


_NC_CACHE = {}


def kernel(L0_attn_query, L0_attn_Wq, L0_attn_bq, L0_attn_Wk, L0_attn_bk, L0_attn_Wv,
           L0_attn_bv, L0_attn_Wu, L0_attn_bu, L0_attn_Wo, L0_attn_bo, L0_attn_ln_w,
           L0_attn_ln_b, L0_ffn_weight, L0_ffn_bias):
    x = np.ascontiguousarray(np.asarray(L0_attn_query, dtype=np.float32))
    B = x.shape[0]
    per = B // NCORES
    weights = {
        "wq": np.ascontiguousarray(L0_attn_Wq, dtype=np.float32),
        "wk": np.ascontiguousarray(L0_attn_Wk, dtype=np.float32),
        "wv": np.ascontiguousarray(L0_attn_Wv, dtype=np.float32),
        "wu": np.ascontiguousarray(L0_attn_Wu, dtype=np.float32),
        "wo": np.ascontiguousarray(L0_attn_Wo, dtype=np.float32),
        "fw": np.ascontiguousarray(L0_ffn_weight, dtype=np.float32),
        "bq": np.ascontiguousarray(np.asarray(L0_attn_bq, dtype=np.float32).reshape(1, D)),
        "bk": np.ascontiguousarray(np.asarray(L0_attn_bk, dtype=np.float32).reshape(1, D)),
        "bv": np.ascontiguousarray(np.asarray(L0_attn_bv, dtype=np.float32).reshape(1, D)),
        "bu": np.ascontiguousarray(np.asarray(L0_attn_bu, dtype=np.float32).reshape(1, D)),
        "bo": np.ascontiguousarray(np.asarray(L0_attn_bo, dtype=np.float32).reshape(1, D)),
        "lnw": np.ascontiguousarray(np.asarray(L0_attn_ln_w, dtype=np.float32).reshape(1, D)),
        "lnb": np.ascontiguousarray(np.asarray(L0_attn_ln_b, dtype=np.float32).reshape(1, D)),
        "fb": np.ascontiguousarray(np.asarray(L0_ffn_bias, dtype=np.float32).reshape(1, D)),
    }
    if per not in _NC_CACHE:
        _NC_CACHE[per] = build_nc(nseq=per)
    nc = _NC_CACHE[per]
    in_maps = []
    for c in range(NCORES):
        m = {"xin": np.ascontiguousarray(x[c * per:(c + 1) * per].reshape(per * SEQLEN, D))}
        m.update(weights)
        in_maps.append(m)
    res = run_bass_kernel_spmd(nc, in_maps, core_ids=list(range(NCORES)))
    out = np.concatenate([r["xout"].reshape(per, SEQLEN, D) for r in res.results], axis=0)
    return out
